# version 04 (final)
# baseline (speedup 1.0000x reference)
import math
import numpy as np
import ml_dtypes

import concourse.bass as bass
import concourse.mybir as mybir
from concourse.bass_utils import run_bass_kernel_spmd

F32 = mybir.dt.float32
BF16 = mybir.dt.bfloat16
AF = mybir.ActivationFunctionType
ALU = mybir.AluOpType
AX = mybir.AxisListType

D = 2048
NH = 16
HD = 128
TOK = 2048
SCALE = 1.0 / math.sqrt(HD)
EPS = 1e-5

BF = ml_dtypes.bfloat16
RG = [[0, 1, 2, 3], [4, 5, 6, 7]]


class Sched:
    ENGINES = ("Y", "G", "T", "V", "S")

    def __init__(self):
        self.scripts = {e: [] for e in self.ENGINES}
        self.cnt = {}
        self.hwm = {}
        self.last_use = {}

    def wait(self, eng, sem, val):
        if not val or val <= 0:
            return
        if self.hwm.get((eng, sem), 0) >= val:
            return
        self.hwm[(eng, sem)] = val
        self.scripts[eng].append(("w", sem, val))

    def op(self, eng, fn):
        self.scripts[eng].append(("o", fn))

    SELF_SEM = {"V": "sV", "S": "sS"}

    def mark(self, eng, fn, sem, n=1, inc=None):
        own = self.SELF_SEM.get(eng)
        if own is not None:
            self.wait(eng, own, self.cnt.get(own, 0))
        self.cnt[sem] = self.cnt.get(sem, 0) + n
        self.scripts[eng].append(("i", fn, sem, n if inc is None else inc))
        return self.cnt[sem]

    def count(self, sem):
        return self.cnt.get(sem, 0)

    def buf_wait(self, eng, key):
        for s, v in self.last_use.get(key, ()):
            self.wait(eng, s, v)

    def buf_mark(self, key, sem, val):
        self.last_use[key] = [(sem, val)]

    def buf_add(self, key, sem, val):
        self.last_use.setdefault(key, []).append((sem, val))

    def fence(self):
        for e in self.ENGINES:
            for s in ("sT", "sV", "sS"):
                self.wait(e, s, self.count(s))


def build_program():
    nc = bass.Bass("TRN2", target_bir_lowering=False,
                   detect_race_conditions=False)

    x_d = nc.dram_tensor("x", [TOK, D], F32, kind="ExternalInput").ap()
    wq_d = nc.dram_tensor("wq", [16 * 16 * 128 * 128], F32, kind="ExternalInput").ap()
    wk_d = nc.dram_tensor("wk", [16 * 16 * 128 * 128], F32, kind="ExternalInput").ap()
    wv_d = nc.dram_tensor("wv", [4 * 16 * 128 * 512], F32, kind="ExternalInput").ap()
    wo_d = nc.dram_tensor("wo", [4 * 16 * 128 * 512], F32, kind="ExternalInput").ap()
    gu_d = nc.dram_tensor("gu", [4 * 16 * 32 * 128 * 128], F32, kind="ExternalInput").ap()
    dn_d = nc.dram_tensor("dn", [4 * 16 * 16 * 128 * 128], F32, kind="ExternalInput").ap()
    idn_d = nc.dram_tensor("idn", [128, 128], BF16, kind="ExternalInput").ap()
    tri_d = nc.dram_tensor("tri", [128, 2048], BF16, kind="ExternalInput").ap()
    sel32_d = nc.dram_tensor("sel32", [128, 64], F32, kind="ExternalInput").ap()
    selb_d = nc.dram_tensor("selb", [128, 64], BF16, kind="ExternalInput").ap()

    y_d = nc.dram_tensor("y", [TOK, D], F32, kind="ExternalOutput").ap()

    q_own = nc.dram_tensor("q_own", [D, TOK], BF16).ap()
    k_own = nc.dram_tensor("k_own", [D, TOK], BF16).ap()
    v_own = nc.dram_tensor("v_own", [NH * TOK, HD], BF16).ap()
    nrm_own = nc.dram_tensor("nrm_own", [1, NH], F32).ap()
    k_all = nc.dram_tensor("k_all", [4 * D, TOK], BF16).ap()
    v_all = nc.dram_tensor("v_all", [4 * NH * TOK, HD], BF16).ap()
    nrm_all = nc.dram_tensor("nrm_all", [4, NH], F32).ap()
    x2_d = nc.dram_tensor("x2d", [TOK, D], BF16).ap()

    sch = Sched()

    from contextlib import ExitStack
    with ExitStack() as stack:
        ent = stack.enter_context
        B = ent(nc.sbuf_tensor("B", [128, 32768], BF16))
        C16 = ent(nc.sbuf_tensor("C16", [128, 32768], BF16))
        C32 = ent(nc.sbuf_tensor("C32", [128, 5120], F32))
        WST = ent(nc.sbuf_tensor("WST", [128, 8192], BF16))
        XA = ent(nc.sbuf_tensor("XA", [128, 4096], BF16))
        TRI = ent(nc.sbuf_tensor("TRI", [128, 2048], BF16))
        SEL32 = ent(nc.sbuf_tensor("SEL32", [128, 64], F32))
        SELB = ent(nc.sbuf_tensor("SELB", [128, 64], BF16))
        IDN = ent(nc.sbuf_tensor("IDN", [128, 128], BF16))
        ONEB = ent(nc.sbuf_tensor("ONEB", [128, 1], BF16))
        ONEF = ent(nc.sbuf_tensor("ONEF", [1, 128], F32))
        BIAS = ent(nc.sbuf_tensor("BIAS", [128, NH], F32))
        EPSC = ent(nc.sbuf_tensor("EPSC", [128, 1], F32))
        NRC = ent(nc.sbuf_tensor("NRC", [1, 64], F32))
        PSA = ent(nc.psum_tensor("PSA", [128, 1024], F32))
        PSB = ent(nc.psum_tensor("PSB", [128, 1024], F32))
        PZ = ent(nc.psum_tensor("PZ", [128, 512], F32))
        PSS = ent(nc.psum_tensor("PSS", [128, 512], F32))
        PMISC = ent(nc.psum_tensor("PMISC", [128, 512], F32))
        PTL = ent(nc.psum_tensor("PTL", [128, 1024], BF16))
        sems = {}
        for sname in ("dY", "dG", "xa0", "xa1", "xg0", "xg1", "w0", "w1", "w2", "w3",
                      "wc0", "wc1", "qh0", "qh1", "kh", "vh", "ko", "vo",
                      "cc", "sT", "sV", "sS"):
            sems[sname] = ent(nc.semaphore(sname))
        block = ent(nc.Block())
        WSEM = ["w0", "w1", "w2", "w3"]
        WCSEM = ["wc0", "wc1"]
        XASEM = ["xa0", "xa1"]
        XGSEM = ["xg0", "xg1"]
        QHSEM = ["qh0", "qh1"]

        c32_tw = C32[:, 0:2048]
        c32_out = C32[:, 2048:4096]
        c32_t = C32[:, 4096:4608]
        stc = [C32[:, 4608 + i:4609 + i] for i in range(10)]
        st_s1, st_s2, st_m, st_v, st_r = stc[0], stc[1], stc[2], stc[3], stc[4]

        sch.mark("Y", lambda e: e.dma_start(out=IDN[:], in_=idn_d[:]), "dY", 16)
        sch.mark("Y", lambda e: e.dma_start(out=TRI[:], in_=tri_d[:]), "dY", 16)
        sch.mark("Y", lambda e: e.dma_start(out=SEL32[:], in_=sel32_d[:]), "dY", 16)
        cst_v = sch.mark(
            "Y", lambda e: e.dma_start(out=SELB[:], in_=selb_d[:]), "dY", 16)

        sch.mark("V", lambda e: e.memset(ONEB[:], 1.0), "sV", 1)
        sch.mark("V", lambda e: e.memset(EPSC[:], EPS), "sV", 1)
        ones_v = sch.mark("V", lambda e: e.memset(ONEF[:], 1.0), "sV", 1)
        sch.wait("T", "sV", ones_v)
        sch.wait("T", "dY", cst_v)
        sch.wait("V", "dY", cst_v)
        sch.wait("S", "dY", cst_v)

        def ln_stats(xtile, dump, producer_sem=None, producer_val=None):
            if producer_sem is not None:
                sch.wait("V", producer_sem, producer_val)
                sch.wait("S", producer_sem, producer_val)

            def _v0(e):
                return e.reduce_sum(st_s1, xtile, axis=AX.X)
            sch.mark("V", _v0, "sV", 1)

            sch.buf_wait("S", ("dump",))

            def _s0(e):
                return e.activation(dump, xtile, AF.Square, accum_out=st_s2)
            sv = sch.mark("S", _s0, "sS", 1)
            sch.buf_mark(("dump",), "sS", sv)
            sch.wait("V", "sS", sv)

            sch.mark("V", lambda e: e.tensor_scalar_mul(st_m, st_s1, 1.0 / D),
                     "sV", 1)
            sch.mark("V", lambda e: e.tensor_scalar_mul(st_v, st_s2, 1.0 / D),
                     "sV", 1)
            sch.mark("V", lambda e: e.tensor_tensor(st_s1, st_m, st_m,
                                                    op=ALU.mult), "sV", 1)
            vv = sch.mark("V", lambda e: e.tensor_sub(st_v, st_v, st_s1),
                          "sV", 1)
            sch.wait("S", "sV", vv)

            def _s1(e):
                return e.activation(st_s2, st_v, AF.Sqrt, bias=EPSC[:])
            sv2 = sch.mark("S", _s1, "sS", 1)
            sch.wait("V", "sS", sv2)

            def _v2(e):
                return e.reciprocal(st_r, st_s2)
            sch.mark("V", _v2, "sV", 1)

        def fm_transposes(src_tile, tt):
            for half in range(2):
                sch.buf_wait("T", ("PTL",))

                def _t(e, half=half, src=src_tile):
                    ii = None
                    for j in range(8):
                        dc = half * 8 + j
                        ii = e.transpose(PTL[:, j * 128:(j + 1) * 128],
                                         src[:, dc * 128:(dc + 1) * 128], IDN[:])
                    return ii
                tv = sch.mark("T", _t, "sT", 1)
                sch.wait("V", "sT", tv)

                def _v(e, half=half, tt=tt):
                    dst = B.rearrange("p (c t) -> p c t", c=16)[
                        :, half * 8:(half + 1) * 8, tt * 128:(tt + 1) * 128]
                    return e.tensor_copy(dst, PTL.rearrange("p (c t) -> p c t", c=8))
                vv = sch.mark("V", _v, "sV", 1)
                sch.buf_mark(("PTL",), "sV", vv)

        sqdump = C16[:, 0:2048]
        h1st = [C16[:, 2048:4096], C16[:, 4096:6144]]

        for tt in range(16):
            p = tt % 2
            xa = XA[:, p * 2048:(p + 1) * 2048]
            sch.buf_wait("G", ("XA", p))

            def _g(e, tt=tt, xa=xa):
                return e.dma_start(out=xa, in_=x_d[tt * 128:(tt + 1) * 128, :])
            xv = sch.mark("G", _g, XGSEM[p], 16)
            ln_stats(xa, sqdump, XGSEM[p], xv)

            hb = h1st[p]
            sch.buf_wait("V", ("h1st", p))

            def _vh(e, xa=xa, hb=hb):
                return e.tensor_scalar(hb, xa, st_m, st_r,
                                       op0=ALU.subtract, op1=ALU.mult)
            hv = sch.mark("V", _vh, "sV", 1)
            sch.buf_mark(("XA", p), "sV", hv)
            sch.buf_add(("XA", p), "sS", sch.count("sS"))
            sch.wait("T", "sV", hv)
            fm_transposes(hb, tt)
            sch.buf_mark(("h1st", p), "sT", sch.count("sT"))

        h1_V = sch.count("sV")
        sch.fence()

        wcol_sl = [C16[:, 0:8192], C16[:, 8192:16384]]
        qkst = [C16[:, 16384:18432], C16[:, 18432:20480]]
        vst = [C16[:, 20480:20992], C16[:, 20992:21504]]
        ndump = C16[:, 22528:24576]

        h1fm = B.rearrange("p (c t) -> p c t", c=16)
        q2m = NRC[0:1, 0:16] if False else None
        q2m = C32[0:1, 4640:4656]
        k2m = C32[0:1, 4656:4672]

        def proj_qk(w_dram, out_dram, out_sem, scale, nrm_row):
            for dc in range(16):
                ws = dc % 4
                sch.buf_wait("G", ("WST", ws))

                def _gw(e, dc=dc, ws=ws, w_dram=w_dram):
                    return e.dma_start(
                        out=WST[:, ws * 2048:(ws + 1) * 2048]
                            .rearrange("p (c f) -> p c f", c=16),
                        in_=w_dram.rearrange("(dc kc p f) -> dc p kc f",
                                             dc=16, kc=16, p=128)[dc])
                wv = sch.mark("G", _gw, WSEM[ws], 16)

                stp = dc % 2
                stage = qkst[stp]
                sch.buf_wait("V", ("qkst", stp))
                for tb in range(4):
                    bank = PSA[:, (tb % 2) * 512:(tb % 2 + 1) * 512]
                    sch.buf_wait("T", ("PSAh", tb % 2))
                    sch.wait("T", WSEM[ws], wv)
                    sch.wait("T", "sV", h1_V)

                    def _t(e, ws=ws, tb=tb, bank=bank):
                        wtile = WST[:, ws * 2048:(ws + 1) * 2048]
                        ii = None
                        for kc in range(16):
                            ii = e.matmul(bank, wtile[:, kc * 128:(kc + 1) * 128],
                                          h1fm[:, kc, tb * 512:(tb + 1) * 512],
                                          start=(kc == 0), stop=(kc == 15))
                        return ii
                    tv = sch.mark("T", _t, "sT", 1)
                    sch.wait("V", "sT", tv)

                    def _v(e, stage=stage, tb=tb, bank=bank):
                        dst = stage[:, tb * 512:(tb + 1) * 512]
                        if scale is None:
                            return e.tensor_copy(dst, bank)
                        return e.tensor_scalar_mul(dst, bank, scale)
                    vv = sch.mark("V", _v, "sV", 1)
                    sch.buf_mark(("PSAh", tb % 2), "sV", vv)
                ev = sch.count("sV")
                sch.buf_mark(("WST", ws), "sV", ev)

                sch.wait("Y", "sV", ev)

                def _y(e, stage=stage, dc=dc, out_dram=out_dram):
                    return e.dma_start(out=out_dram[dc * 128:(dc + 1) * 128, :],
                                       in_=stage)
                yv = sch.mark("Y", _y, out_sem, 16)
                sch.buf_mark(("qkst", stp), out_sem, yv)

                sch.wait("S", "sV", ev)
                sch.buf_wait("S", ("ndump",))

                def _sq(e, stage=stage):
                    return e.activation(ndump, stage, AF.Square)
                sv = sch.mark("S", _sq, "sS", 1)
                sch.buf_add(("qkst", stp), "sS", sv)
                sch.wait("T", "sS", sv)
                sch.buf_wait("T", ("PSBn",))

                def _tn(e):
                    ii = None
                    for q4 in range(4):
                        lo = (q4 % 2) * 512
                        ii = e.matmul(PSB[0:1, lo:lo + 512], ONEB[:],
                                      ndump[:, q4 * 512:(q4 + 1) * 512],
                                      start=(q4 < 2), stop=(q4 >= 2))
                    return ii
                tnv = sch.mark("T", _tn, "sT", 1)
                sch.wait("V", "sT", tnv)

                def _vn(e, dc=dc, nrm_row=nrm_row):
                    return e.reduce_max(nrm_row[0:1, dc:dc + 1],
                                        PSB[0:1, 0:1024], axis=AX.X)
                vnv = sch.mark("V", _vn, "sV", 1)
                sch.buf_mark(("PSBn",), "sV", vnv)
                sch.buf_mark(("ndump",), "sV", vnv)

        proj_qk(wk_d, k_own, "ko", None, k2m)
        k_store_v = sch.count("ko")

        for vb in range(4):
            wcp = vb % 2
            sch.buf_wait("G", ("WCOL", wcp))

            def _gv(e, vb=vb, wcp=wcp):
                return e.dma_start(
                    out=wcol_sl[wcp].rearrange("p (c f) -> p c f", c=16),
                    in_=wv_d.rearrange("(vb kc p f) -> vb p kc f",
                                       vb=4, kc=16, p=128)[vb])
            wv = sch.mark("G", _gv, WCSEM[wcp], 16)
            for tt in range(16):
                bank = PSA[:, (tt % 2) * 512:(tt % 2 + 1) * 512]
                sch.buf_wait("T", ("PSAh", tt % 2))
                sch.wait("T", WCSEM[wcp], wv)
                sch.wait("T", "sV", h1_V)

                def _t(e, wcp=wcp, tt=tt, bank=bank):
                    ii = None
                    for kc in range(16):
                        ii = e.matmul(bank,
                                      h1fm[:, kc, tt * 128:(tt + 1) * 128],
                                      wcol_sl[wcp][:, kc * 512:(kc + 1) * 512],
                                      start=(kc == 0), stop=(kc == 15))
                    return ii
                tv = sch.mark("T", _t, "sT", 1)
                vsp = tt % 2
                sch.buf_wait("V", ("vst", vsp))
                sch.wait("V", "sT", tv)

                def _v(e, vsp=vsp, bank=bank):
                    return e.tensor_copy(vst[vsp], bank)
                vv = sch.mark("V", _v, "sV", 1)
                sch.buf_mark(("PSAh", tt % 2), "sV", vv)
                sch.wait("Y", "sV", vv)

                def _y(e, vb=vb, tt=tt, vsp=vsp):
                    dst = v_own.rearrange("(h t) d -> h t d", h=NH)[
                        vb * 4:(vb + 1) * 4, tt * 128:(tt + 1) * 128, :]
                    return e.dma_start(
                        out=dst.rearrange("j p d -> p j d"),
                        in_=vst[vsp].rearrange("p (j d) -> p j d", j=4))
                yv = sch.mark("Y", _y, "vo", 16)
                sch.buf_mark(("vst", vsp), "vo", yv)
            sch.buf_mark(("WCOL", wcp), "sT", sch.count("sT"))
        v_store_v = sch.count("vo")

        sch.wait("Y", "sV", sch.count("sV"))

        def _yn(e):
            return e.dma_start(out=nrm_own[0:1, :], in_=k2m)
        nst_v = sch.mark("Y", _yn, "dY", 16)

        sch.wait("G", "ko", k_store_v)
        sch.wait("G", "vo", v_store_v)
        sch.wait("G", "dY", nst_v)

        def _cc1(e):
            return e.collective_compute("AllGather", ALU.bypass,
                                        replica_groups=RG,
                                        ins=[k_own.opt()], outs=[k_all.opt()])
        sch.mark("G", _cc1, "cc", 1)

        def _cc2(e):
            return e.collective_compute("AllGather", ALU.bypass,
                                        replica_groups=RG,
                                        ins=[v_own.opt()], outs=[v_all.opt()])
        sch.mark("G", _cc2, "cc", 1)

        def _cc3(e):
            return e.collective_compute("AllGather", ALU.bypass,
                                        replica_groups=RG,
                                        ins=[nrm_own.opt()], outs=[nrm_all.opt()])
        cc_v = sch.mark("G", _cc3, "cc", 1)

        proj_qk(wq_d, q_own, "dY", SCALE, q2m)
        q_store_v = sch.count("dY")

        sch.wait("Y", "cc", cc_v)

        def _yl(e):
            return e.dma_start(out=NRC[0:1, 0:64],
                               in_=nrm_all.rearrange("r h -> (r h)"))
        nl_v = sch.mark("Y", _yl, "dY", 16)
        sch.wait("V", "dY", nl_v)

        sch.mark("V", lambda e: e.reduce_max(
            k2m, NRC[0:1, 0:64].rearrange("o (r h) -> o h r", r=4),
            axis=AX.X), "sV", 1)
        vb_v = sch.mark("V", lambda e: e.tensor_mul(k2m, k2m, q2m), "sV", 1)
        sch.wait("S", "sV", vb_v)

        def _sb16(e):
            return e.activation(q2m, k2m, AF.Sqrt)
        sb_v = sch.mark("S", _sb16, "sS", 1)
        sch.wait("V", "sS", sb_v)

        def _vneg(e):
            return e.tensor_scalar_mul(q2m, q2m, -1.0)
        vn_v = sch.mark("V", _vneg, "sV", 1)
        sch.wait("T", "sV", vn_v)
        sch.buf_wait("T", ("PMISC",))

        def _tb16(e):
            return e.matmul(PMISC[:, 0:16], ONEF[:], q2m, start=True, stop=True)
        tb_v = sch.mark("T", _tb16, "sT", 1)
        sch.wait("V", "sT", tb_v)

        def _vbb(e):
            return e.tensor_copy(BIAS[:], PMISC[:, 0:16])
        bias_v = sch.mark("V", _vbb, "sV", 1)
        sch.buf_mark(("PMISC",), "sV", bias_v)

        sch.fence()

        KH = C16[:, 0:8192]
        VH = C16[:, 8192:16384]
        QH = [C16[:, 16384:18432], C16[:, 18432:20480]]
        KO = [C16[:, 20480:22528], C16[:, 22528:24576]]
        VO = [C16[:, 24576:26624], C16[:, 26624:28672]]
        PT = [C16[:, 28672:29696], C16[:, 29696:30720]]
        VM = C16[:, 30720:32768]

        def span_rank(s):
            if s < 4:
                return s
            if s < 8:
                return 7 - s
            if s < 12:
                return s - 8
            return 15 - s

        def load_kh(h, spans):
            v = None
            for s in spans:
                def _y(e, h=h, s=s):
                    r = span_rank(s)
                    sl = s // 4
                    return e.dma_start(
                        out=KH[:, s * 512:(s + 1) * 512],
                        in_=k_all[r * D + h * 128:r * D + (h + 1) * 128,
                                  sl * 512:(sl + 1) * 512])
                v = sch.mark("Y", _y, "kh", 16)
            return v

        def load_vh(h, spans):
            v = None
            for s in spans:
                def _y(e, h=h, s=s):
                    r = span_rank(s)
                    sl = s // 4
                    base = r * NH * TOK + h * TOK + sl * 512
                    return e.dma_start(
                        out=VH[:, s * 512:(s + 1) * 512]
                            .rearrange("p (kb d) -> p kb d", kb=4),
                        in_=v_all[base:base + 512, :]
                            .rearrange("(kb p) d -> p kb d", p=128))
                v = sch.mark("Y", _y, "vh", 16)
            return v

        def load_small(h, which):
            p = h % 2
            if which == "q":
                sch.wait("Y", "dY", q_store_v)
                sch.buf_wait("Y", ("QH", p))

                def _y(e, h=h, p=p):
                    return e.dma_start(out=QH[p],
                                       in_=q_own[h * 128:(h + 1) * 128, :])
                return sch.mark("Y", _y, QHSEM[p], 16)
            if which == "ko":
                sch.wait("Y", "ko", k_store_v)
                sch.buf_wait("Y", ("KO", p))

                def _y(e, h=h, p=p):
                    return e.dma_start(out=KO[p],
                                       in_=k_own[h * 128:(h + 1) * 128, :])
                return sch.mark("Y", _y, "ko", 16)
            sch.wait("Y", "vo", v_store_v)
            sch.buf_wait("Y", ("VO", p))

            def _y(e, h=h, p=p):
                return e.dma_start(
                    out=VO[p].rearrange("p (kb d) -> p kb d", kb=16),
                    in_=v_own[h * TOK:(h + 1) * TOK, :]
                        .rearrange("(kb p) d -> p kb d", p=128))
            return sch.mark("Y", _y, "vo", 16)

        sch.wait("Y", "cc", cc_v)
        load_kh(0, range(16))
        load_vh(0, range(16))
        qh_v = {0: load_small(0, "q")}
        ko_v = {0: load_small(0, "ko")}
        vo_v = {0: load_small(0, "vo")}

        for h in range(NH):
            p = h % 2
            sch.wait("T", "kh", sch.count("kh"))
            sch.wait("T", "vh", sch.count("vh"))
            sch.wait("T", QHSEM[p], qh_v[h])
            sch.wait("T", "ko", ko_v[h])
            sch.wait("T", "vo", vo_v[h])
            sch.wait("T", "sV", bias_v)
            sch.wait("S", "sV", bias_v)
            sch.wait("V", "vh", sch.count("vh"))

            for slot in (3, 2, 1, 0):
                nsc = slot + 1
                n_kgroups = nsc * 8 + 2

                sch.buf_wait("V", ("VM",))

                for kb in range(16):
                    def _vm(e, slot=slot, kb=kb):
                        src = VH[:, (slot * 16 + kb) * 128:
                                 (slot * 16 + kb + 1) * 128]
                        return e.tensor_scalar_mul(
                            VM[:, kb * 128:(kb + 1) * 128], src,
                            SEL32[:, slot * 16 + kb:slot * 16 + kb + 1])
                    vm_v = sch.mark("V", _vm, "sV", 1)
                sch.wait("T", "sV", vm_v)
                sch.buf_wait("T", ("PZ",))
                sch.buf_wait("T", ("PSS",))

                kg_idx = 0
                for sc in range(nsc):
                    last_sc = (sc == slot)
                    for kg in range(8):
                        par = kg_idx % 2
                        SPS = PSA if par == 0 else PSB
                        sch.buf_wait("T", ("SPS", par))

                        def _ts(e, sc=sc, kg=kg, slot=slot, SPS=SPS, p=p):
                            ii = None
                            for bb in range(2):
                                kb = sc * 16 + kg * 2 + bb
                                ii = e.matmul(
                                    SPS[:, bb * 512:(bb + 1) * 512],
                                    KH[:, kb * 128:(kb + 1) * 128],
                                    QH[p][:, slot * 512:(slot + 1) * 512],
                                    start=True, stop=True)
                            return ii
                        tsv = sch.mark("T", _ts, "sT", 1)
                        sch.wait("S", "sT", tsv)
                        ptp = kg_idx % 2
                        sch.buf_wait("S", ("PT", ptp))

                        def _se(e, SPS=SPS, ptp=ptp, h=h):
                            return e.activation(PT[ptp], SPS[:, 0:1024], AF.Exp,
                                                bias=BIAS[:, h:h + 1])
                        sev = sch.mark("S", _se, "sS", 1)
                        sch.buf_mark(("SPS", par), "sS", sev)
                        sch.wait("T", "sS", sev)

                        def _tpv(e, kg=kg, slot=slot, ptp=ptp, p=p,
                                 last_sc=last_sc, first=(kg_idx == 0)):
                            ii = None
                            for bb in range(2):
                                lkb = kg * 2 + bb
                                if last_sc:
                                    vsl = VM[:, lkb * 128:(lkb + 1) * 128]
                                    selc = SELB[:, slot * 16 + lkb:
                                                slot * 16 + lkb + 1]
                                else:
                                    kb = 0
                                    vsl = VH[:, (0 * 16 + 0) * 128:0]
                                    selc = ONEB[:]
                                st = first and bb == 0
                                e.matmul(PZ[:], vsl,
                                         PT[ptp][:, bb * 512:(bb + 1) * 512],
                                         start=st, stop=False)
                                ii = e.matmul(PSS[0:1, :], selc,
                                              PT[ptp][:, bb * 512:(bb + 1) * 512],
                                              start=st, stop=False)
                            return ii

                        def _tpv(e, sc=sc, kg=kg, slot=slot, ptp=ptp,
                                 last_sc=last_sc, first=(kg_idx == 0)):
                            ii = None
                            for bb in range(2):
                                kb = sc * 16 + kg * 2 + bb
                                lkb = kg * 2 + bb
                                if last_sc:
                                    vsl = VM[:, lkb * 128:(lkb + 1) * 128]
                                    selc = SELB[:, slot * 16 + lkb:
                                                slot * 16 + lkb + 1]
                                else:
                                    vsl = VH[:, kb * 128:(kb + 1) * 128]
                                    selc = ONEB[:]
                                st = first and bb == 0
                                e.matmul(PZ[:], vsl,
                                         PT[ptp][:, bb * 512:(bb + 1) * 512],
                                         start=st, stop=False)
                                ii = e.matmul(PSS[0:1, :], selc,
                                              PT[ptp][:, bb * 512:(bb + 1) * 512],
                                              start=st, stop=False)
                            return ii
                        tpv = sch.mark("T", _tpv, "sT", 1)
                        sch.buf_mark(("PT", ptp), "sT", tpv)
                        kg_idx += 1

                for og in range(2):
                    par = kg_idx % 2
                    SPS = PSA if par == 0 else PSB
                    sch.buf_wait("T", ("SPS", par))

                    def _ts(e, og=og, slot=slot, SPS=SPS, p=p):
                        ii = None
                        for bb in range(2):
                            kb = og * 2 + bb
                            ii = e.matmul(
                                SPS[:, bb * 512:(bb + 1) * 512],
                                KO[p][:, slot * 512 + kb * 128:
                                       slot * 512 + (kb + 1) * 128],
                                QH[p][:, slot * 512:(slot + 1) * 512],
                                start=True, stop=True)
                        return ii
                    tsv = sch.mark("T", _ts, "sT", 1)
                    sch.wait("S", "sT", tsv)
                    ptp = kg_idx % 2
                    sch.buf_wait("S", ("PT", ptp))

                    def _se(e, SPS=SPS, ptp=ptp, h=h):
                        return e.activation(PT[ptp], SPS[:, 0:1024], AF.Exp,
                                            bias=BIAS[:, h:h + 1])
                    sev = sch.mark("S", _se, "sS", 1)
                    sch.buf_mark(("SPS", par), "sS", sev)
                    sch.wait("V", "sS", sev)

                    def _vt(e, ptp=ptp, og=og):
                        return e.tensor_mul(PT[ptp], PT[ptp],
                                            TRI[:, og * 1024:(og + 1) * 1024])
                    vtv = sch.mark("V", _vt, "sV", 1)
                    sch.wait("T", "sV", vtv)

                    def _tpv(e, og=og, slot=slot, ptp=ptp, p=p,
                             last=(og == 1)):
                        ii = None
                        for bb in range(2):
                            kb = og * 2 + bb
                            lastg = last and (bb == 1)
                            vsl = VO[p][:, (slot * 4 + kb) * 128:
                                        (slot * 4 + kb + 1) * 128]
                            e.matmul(PZ[:], vsl,
                                     PT[ptp][:, bb * 512:(bb + 1) * 512],
                                     start=False, stop=lastg)
                            ii = e.matmul(PSS[0:1, :], ONEB[:],
                                          PT[ptp][:, bb * 512:(bb + 1) * 512],
                                          start=False, stop=lastg)
                        return ii
                    tpv = sch.mark("T", _tpv, "sT", 1)
                    sch.buf_mark(("PT", ptp), "sT", tpv)
                    kg_idx += 1

                last_t = sch.count("sT")

                if h + 1 < NH:
                    sch.wait("Y", "sT", last_t)
                    load_kh(h + 1, range(slot * 4, slot * 4 + 4))
                    load_vh(h + 1, range(slot * 4, slot * 4 + 4))
                    if slot == 3:
                        qh_v[h + 1] = load_small(h + 1, "q")
                        ko_v[h + 1] = load_small(h + 1, "ko")
                        vo_v[h + 1] = load_small(h + 1, "vo")

                sch.wait("V", "sT", last_t)

                def _vr(e):
                    return e.reciprocal(c32_t[0:1, 0:512], PSS[0:1, :])
                vr_v = sch.mark("V", _vr, "sV", 1)
                sch.wait("T", "sV", vr_v)
                sch.buf_wait("T", ("PMISC",))

                def _tbc(e):
                    return e.matmul(PMISC[:], ONEF[:], c32_t[0:1, 0:512],
                                    start=True, stop=True)
                tbc_v = sch.mark("T", _tbc, "sT", 1)
                sch.wait("V", "sT", tbc_v)

                sch.mark("V", lambda e: e.tensor_copy(c32_t[:, 0:512],
                                                      PMISC[:]), "sV", 1)

                def _vz(e, h=h, slot=slot):
                    return e.tensor_mul(
                        B[:, h * 2048 + slot * 512:h * 2048 + (slot + 1) * 512],
                        PZ[:], c32_t[:, 0:512])
                vz_v = sch.mark("V", _vz, "sV", 1)
                sch.buf_mark(("PMISC",), "sV", vz_v)
                sch.buf_mark(("PZ",), "sV", vz_v)
                sch.buf_mark(("PSS",), "sV", vz_v)
                sch.buf_mark(("VM",), "sT", last_t)

            sch.buf_mark(("QH", p), "sT", sch.count("sT"))
            sch.buf_mark(("KO", p), "sT", sch.count("sT"))
            sch.buf_mark(("VO", p), "sT", sch.count("sT"))

        sch.fence()

        wcolA = [C16[:, 0:8192], C16[:, 8192:16384]]
        attq = C16[:, 16384:24576]
        sqd5 = C16[:, 24576:26624]
        tst5 = C16[:, 26624:28672]
        x2st = C16[:, 28672:30720]

        for quarter in range(4):
            for ob in range(4):
                wcp = ob % 2
                sch.buf_wait("G", ("WCOLA", wcp))

                def _gw(e, ob=ob, wcp=wcp):
                    return e.dma_start(
                        out=wcolA[wcp].rearrange("p (c f) -> p c f", c=16),
                        in_=wo_d.rearrange("(ob kc p f) -> ob p kc f",
                                           ob=4, kc=16, p=128)[ob])
                wv = sch.mark("G", _gw, WCSEM[wcp], 16)
                for ti in range(4):
                    tt = quarter * 4 + ti
                    bank = PSA[:, (ti % 2) * 512:(ti % 2 + 1) * 512]
                    sch.buf_wait("T", ("PSAh", ti % 2))
                    sch.wait("T", WCSEM[wcp], wv)

                    def _t(e, tt=tt, wcp=wcp, bank=bank):
                        ii = None
                        for hh in range(16):
                            ii = e.matmul(bank,
                                          B[:, hh * 2048 + tt * 128:
                                            hh * 2048 + (tt + 1) * 128],
                                          wcolA[wcp][:, hh * 512:(hh + 1) * 512],
                                          start=(hh == 0), stop=(hh == 15))
                        return ii
                    tv = sch.mark("T", _t, "sT", 1)
                    sch.wait("V", "sT", tv)
                    sch.buf_wait("V", ("attq", ti))

                    def _v(e, ti=ti, ob=ob, bank=bank):
                        return e.tensor_copy(
                            attq[:, ti * 2048 + ob * 512:
                                 ti * 2048 + (ob + 1) * 512], bank)
                    vv = sch.mark("V", _v, "sV", 1)
                    sch.buf_mark(("PSAh", ti % 2), "sV", vv)
                sch.buf_mark(("WCOLA", wcp), "sT", sch.count("sT"))

            for ti in range(4):
                tt = quarter * 4 + ti
                p = tt % 2
                xa = XA[:, p * 2048:(p + 1) * 2048]
                sch.buf_wait("G", ("XA", p))

                def _gx(e, tt=tt, xa=xa):
                    return e.dma_start(out=xa, in_=x_d[tt * 128:(tt + 1) * 128, :])
                xv = sch.mark("G", _gx, XGSEM[p], 16)

                at = attq[:, ti * 2048:(ti + 1) * 2048]
                sch.wait("S", "sV", sch.count("sV"))
                ln_stats(at, sqd5)
                sch.buf_add(("attq", ti), "sS", sch.count("sS"))
                sch.buf_wait("V", ("x2st",))
                sch.wait("V", XGSEM[p], xv)

                def _vx1(e, at=at):
                    return e.tensor_scalar(tst5, at, st_m, st_r,
                                           op0=ALU.subtract, op1=ALU.mult)
                sch.mark("V", _vx1, "sV", 1)

                def _vx(e, xa=xa):
                    return e.tensor_add(x2st, xa, tst5)
                xv2 = sch.mark("V", _vx, "sV", 1)
                sch.buf_mark(("XA", p), "sV", xv2)
                sch.buf_mark(("attq", ti), "sV", xv2)
                sch.wait("Y", "sV", xv2)

                def _yx(e, tt=tt):
                    return e.dma_start(out=x2_d[tt * 128:(tt + 1) * 128, :],
                                       in_=x2st)
                yv = sch.mark("Y", _yx, "dY", 16)
                sch.buf_mark(("x2st",), "dY", yv)

        x2_done = sch.count("dY")
        sch.fence()

        sqd6 = C16[:, 0:2048]
        h2st = [C16[:, 2048:4096], C16[:, 4096:6144]]
        for tt in range(16):
            p = tt % 2
            xa = XA[:, p * 2048:(p + 1) * 2048]
            sch.buf_wait("Y", ("XA", p))
            sch.wait("Y", "dY", x2_done)

            def _yx(e, tt=tt, xa=xa):
                return e.dma_start(out=xa, in_=x2_d[tt * 128:(tt + 1) * 128, :])
            xv = sch.mark("Y", _yx, XASEM[p], 16)
            ln_stats(xa, sqd6, XASEM[p], xv)
            hb = h2st[p]
            sch.buf_wait("V", ("h2st", p))

            def _vh(e, xa=xa, hb=hb):
                return e.tensor_scalar(hb, xa, st_m, st_r,
                                       op0=ALU.subtract, op1=ALU.mult)
            hv = sch.mark("V", _vh, "sV", 1)
            sch.buf_mark(("XA", p), "sV", hv)
            sch.buf_add(("XA", p), "sS", sch.count("sS"))
            sch.wait("T", "sV", hv)
            fm_transposes(hb, tt)
            sch.buf_mark(("h2st", p), "sT", sch.count("sT"))

        h2_V = sch.count("sV")
        h2_T = sch.count("sT")
        h2fm = B.rearrange("p (c t) -> p c t", c=16)
        sch.fence()

        inter = C16
        gu_r = gu_d.rearrange("(e g c p f) -> e g p c f", e=4, g=16, c=32, p=128)
        pair_i = 0
        for e_i in range(4):
            for g in range(16):
                ws = pair_i % 2
                sch.buf_wait("G", ("WSTP", ws))

                def _gw(e, e_i=e_i, g=g, ws=ws):
                    return e.dma_start(
                        out=WST[:, ws * 4096:(ws + 1) * 4096]
                            .rearrange("p (c f) -> p c f", c=32),
                        in_=gu_r[e_i, g])
                wv = sch.mark("G", _gw, WSEM[ws], 16)
                par = pair_i % 2
                PS = PSA if par == 0 else PSB
                sch.buf_wait("T", ("PSF", par))
                sch.wait("T", WSEM[ws], wv)
                sch.wait("T", "sV", h2_V)
                sch.wait("T", "sT", h2_T)

                def _t(e, e_i=e_i, ws=ws, PS=PS):
                    wt = WST[:, ws * 4096:(ws + 1) * 4096]
                    ii = None
                    for kc in range(16):
                        rhs = h2fm[:, kc].rearrange(
                            "p (t four) -> p four t", four=4)[:, e_i, :]
                        e.matmul(PS[:, 0:512], wt[:, kc * 128:(kc + 1) * 128],
                                 rhs, start=(kc == 0), stop=(kc == 15))
                        ii = e.matmul(PS[:, 512:1024],
                                      wt[:, 2048 + kc * 128:2048 + (kc + 1) * 128],
                                      rhs, start=(kc == 0), stop=(kc == 15))
                    return ii
                tv = sch.mark("T", _t, "sT", 1)
                sch.buf_mark(("WSTP", ws), "sT", tv)
                sch.wait("S", "sT", tv)

                def _ss(e, PS=PS):
                    return e.activation(c32_t[:, 0:512], PS[:, 0:512], AF.Silu)
                sv = sch.mark("S", _ss, "sS", 1)
                sch.wait("V", "sS", sv)

                def _vi(e, e_i=e_i, g=g, PS=PS):
                    dst = inter.rearrange("p (c t four) -> p c four t",
                                          c=16, four=4)[:, g, e_i, :]
                    return e.tensor_mul(dst, c32_t[:, 0:512], PS[:, 512:1024])
                vv = sch.mark("V", _vi, "sV", 1)
                sch.buf_mark(("PSF", par), "sV", vv)
                pair_i += 1

        inter_V = sch.count("sV")
        sch.fence()

        dn_r = dn_d.rearrange("(e oc kc p f) -> e oc p kc f",
                              e=4, oc=16, kc=16, p=128)
        blk_i = 0
        for e_i in range(4):
            for oc in range(16):
                ws = blk_i % 4
                sch.buf_wait("G", ("WST4", ws))

                def _gw(e, e_i=e_i, oc=oc, ws=ws):
                    return e.dma_start(
                        out=WST[:, ws * 2048:(ws + 1) * 2048]
                            .rearrange("p (c f) -> p c f", c=16),
                        in_=dn_r[e_i, oc])
                wv = sch.mark("G", _gw, WSEM[ws], 16)
                par = blk_i % 2
                PS = PSA if par == 0 else PSB
                sch.buf_wait("T", ("PSF", par))
                sch.wait("T", WSEM[ws], wv)
                sch.wait("T", "sV", inter_V)

                def _t(e, e_i=e_i, ws=ws, PS=PS):
                    ii = None
                    for ic in range(16):
                        rhs = inter.rearrange("p (c t four) -> p c four t",
                                              c=16, four=4)[:, ic, e_i, :]
                        ii = e.matmul(PS[:, 0:512],
                                      WST[:, ws * 2048 + ic * 128:
                                          ws * 2048 + (ic + 1) * 128],
                                      rhs, start=(ic == 0), stop=(ic == 15))
                    return ii
                tv = sch.mark("T", _t, "sT", 1)
                sch.buf_mark(("WST4", ws), "sT", tv)
                sch.wait("V", "sT", tv)

                def _v(e, e_i=e_i, oc=oc, PS=PS):
                    dst = B.rearrange("p (c t four) -> p c four t",
                                      c=16, four=4)[:, oc, e_i, :]
                    return e.tensor_copy(dst, PS[:, 0:512])
                vv = sch.mark("V", _v, "sV", 1)
                sch.buf_mark(("PSF", par), "sV", vv)
                blk_i += 1

        ffn_V = sch.count("sV")
        sch.fence()

        ffntt = [C16[:, 0:2048], C16[:, 2048:4096]]
        sqd9 = C16[:, 4096:6144]
        for tt in range(16):
            p = tt % 2
            fb = ffntt[p]
            sch.buf_wait("T", ("ffntt", p))
            for half in range(2):
                sch.buf_wait("T", ("PTL",))
                sch.wait("T", "sV", ffn_V)

                def _t(e, half=half, tt=tt):
                    ii = None
                    for j in range(8):
                        dc = half * 8 + j
                        ii = e.transpose(PTL[:, j * 128:(j + 1) * 128],
                                         B[:, dc * 2048 + tt * 128:
                                           dc * 2048 + (tt + 1) * 128], IDN[:])
                    return ii
                tv = sch.mark("T", _t, "sT", 1)
                sch.wait("V", "sT", tv)

                def _v(e, half=half, fb=fb):
                    return e.tensor_copy(
                        fb.rearrange("p (c t) -> p c t", c=16)[
                            :, half * 8:(half + 1) * 8, :],
                        PTL.rearrange("p (c t) -> p c t", c=8))
                vv = sch.mark("V", _v, "sV", 1)
                sch.buf_mark(("PTL",), "sV", vv)

            xa = XA[:, p * 2048:(p + 1) * 2048]
            sch.buf_wait("Y", ("XA", p))

            def _yx(e, tt=tt, xa=xa):
                return e.dma_start(out=xa, in_=x2_d[tt * 128:(tt + 1) * 128, :])
            xv = sch.mark("Y", _yx, XASEM[p], 16)
            sch.wait("S", "sV", sch.count("sV"))
            ln_stats(fb, sqd9)
            sch.wait("V", XASEM[p], xv)
            sch.buf_wait("V", ("c32out",))

            def _vo1(e, fb=fb):
                return e.tensor_scalar(c32_tw, fb, st_m, st_r,
                                       op0=ALU.subtract, op1=ALU.mult)
            sch.mark("V", _vo1, "sV", 1)

            def _vo(e, xa=xa):
                return e.tensor_add(c32_out, xa, c32_tw)
            ov = sch.mark("V", _vo, "sV", 1)
            sch.buf_mark(("XA", p), "sV", ov)
            sch.buf_mark(("ffntt", p), "sV", ov)
            sch.wait("Y", "sV", ov)

            def _yo(e, tt=tt):
                return e.dma_start(out=y_d[tt * 128:(tt + 1) * 128, :],
                                   in_=c32_out)
            yv = sch.mark("Y", _yo, "dY", 16)
            sch.buf_mark(("c32out",), "dY", yv)

        sch.wait("Y", "dY", sch.count("dY"))

        def replay(script):
            def body(eng):
                for item in script:
                    if item[0] == "w":
                        eng.wait_ge(sems[item[1]], item[2])
                    elif item[0] == "o":
                        item[1](eng)
                    else:
                        item[1](eng).then_inc(sems[item[2]], item[3])
            return body

        block.sync(replay(sch.scripts["Y"]))
        block.gpsimd(replay(sch.scripts["G"]))
        block.tensor(replay(sch.scripts["T"]))
        block.vector(replay(sch.scripts["V"]))
        block.scalar(replay(sch.scripts["S"]))

    return nc


def _spans_for_core(i):
    return [i, 7 - i, 8 + i, 15 - i]


def _host_blocks(Wq, Wk, Wv, Wo, gu, dn):
    def blk16(WT):
        return np.ascontiguousarray(
            WT.reshape(16, 128, 16, 128).transpose(2, 0, 1, 3))

    def blk512(WT):
        return np.ascontiguousarray(
            WT.reshape(16, 128, 4, 512).transpose(2, 0, 1, 3))

    wq_b = blk16(np.ascontiguousarray(Wq.T)).ravel()
    wk_b = blk16(np.ascontiguousarray(Wk.T)).ravel()
    wv_b = blk512(np.ascontiguousarray(Wv.T)).ravel()
    wo_b = blk512(np.ascontiguousarray(Wo.T)).ravel()
    g4 = gu.reshape(4, 16, 128, 32, 128).transpose(0, 3, 1, 2, 4)
    gu_b = np.empty((4, 16, 2, 16, 128, 128), dtype=np.float32)
    gu_b[:, :, 0] = g4[:, 0:16]
    gu_b[:, :, 1] = g4[:, 16:32]
    gu_b = np.ascontiguousarray(gu_b).ravel()
    dn_b = np.ascontiguousarray(
        dn.reshape(4, 16, 128, 16, 128).transpose(0, 3, 1, 2, 4)).ravel()
    return wq_b, wk_b, wv_b, wo_b, gu_b, dn_b


def _host_masks():
    tri = np.zeros((128, 2048), dtype=BF)
    kk = np.arange(128)[:, None]
    q = np.arange(512)[None, :]
    for b in range(4):
        tri[:, b * 512:(b + 1) * 512] = (b * 128 + kk <= q).astype(BF)
    ident = np.eye(128, dtype=BF)
    return tri, ident


def _host_sel(core):
    spans = _spans_for_core(core % 4)
    sel = np.zeros((128, 64), dtype=np.float32)
    for slot in range(4):
        s_own = spans[slot]
        for kb in range(16):
            s_abs = slot * 4 + kb // 4
            sel[:, slot * 16 + kb] = 1.0 if s_abs < s_own else 0.0
    return sel, sel.astype(BF)


def kernel(L0_attn_x, L0_attn_Wq, L0_attn_Wk, L0_attn_Wv, L0_attn_Wo,
           L0_ffn_gate_up_proj, L0_ffn_down_proj):
    x = np.asarray(L0_attn_x, dtype=np.float32)
    Wq = np.asarray(L0_attn_Wq, dtype=np.float32)
    Wk = np.asarray(L0_attn_Wk, dtype=np.float32)
    Wv = np.asarray(L0_attn_Wv, dtype=np.float32)
    Wo = np.asarray(L0_attn_Wo, dtype=np.float32)
    gu = np.asarray(L0_ffn_gate_up_proj, dtype=np.float32)
    dn = np.asarray(L0_ffn_down_proj, dtype=np.float32)

    wq_b, wk_b, wv_b, wo_b, gu_b, dn_b = _host_blocks(Wq, Wk, Wv, Wo, gu, dn)
    tri, ident = _host_masks()

    nc = build_program()

    in_maps = []
    for c in range(8):
        b = c // 4
        spans = _spans_for_core(c % 4)
        xs = np.ascontiguousarray(
            np.concatenate([x[b, s * 512:(s + 1) * 512] for s in spans], axis=0))
        sel32, selb = _host_sel(c)
        in_maps.append({
            "x": xs, "wq": wq_b, "wk": wk_b, "wv": wv_b, "wo": wo_b,
            "gu": gu_b, "dn": dn_b, "idn": ident, "tri": tri,
            "sel32": sel32, "selb": selb,
        })

    res = run_bass_kernel_spmd(nc, in_maps, core_ids=list(range(8)))

    out = np.empty((2, 8192, D), dtype=np.float32)
    for c in range(8):
        b = c // 4
        spans = _spans_for_core(c % 4)
        yc = res.results[c]["y"]
        for j, s in enumerate(spans):
            out[b, s * 512:(s + 1) * 512] = yc[j * 512:(j + 1) * 512]
    return out
